# version 14
# baseline (speedup 1.0000x reference)
import numpy as np
import ml_dtypes
from contextlib import ExitStack

import concourse.bass as bass
import concourse.mybir as mybir
from concourse.bass_utils import run_bass_kernel_spmd

F32 = mybir.dt.float32
BF16 = mybir.dt.bfloat16
I32 = mybir.dt.int32
AF = mybir.ActivationFunctionType
ALU = mybir.AluOpType
AX = mybir.AxisListType

S = 4096
H = 4096
DKV = 1024
INTER = 11008
TOWN = 1024
NHB = H // 128
NIT = INTER // 128
EXP_SHIFT = 20.0
ISQ = 1.0 / float(np.sqrt(128.0))

ENGINES = ("sync", "gpsimd", "tensor", "scalar", "vector")


class Rec:

    def __init__(self):
        self.q = {e: [] for e in ENGINES}

    def op(self, eng, method, *args, waits=(), inc=None, incv=1, **kw):
        self.q[eng].append((method, args, kw, tuple(waits), inc, incv))

    def replay_into(self, blk):
        def mk(ops):
            def body(e):
                for (m, args, kw, waits, inc, incv) in ops:
                    for (s, v) in waits:
                        if v > 0:
                            e.wait_ge(s, v)
                    if m is None:
                        continue
                    ins = getattr(e, m)(*args, **kw)
                    if inc is not None:
                        ins.then_inc(inc, incv)
            return body

        for name in ENGINES:
            ops = self.q[name]
            if ops:
                getattr(blk, name)(mk(ops))


class Ring:

    def __init__(self, semfn, name, nslots):
        self.sems = [semfn(f"{name}{k}") for k in range(nslots)]
        self.cnt = [0] * nslots

    def tick(self, slot, n=16):
        self.cnt[slot] += n
        return self.sems[slot], self.cnt[slot]

    def val(self, slot):
        return (self.sems[slot], self.cnt[slot])

    def finals(self):
        return [(s, c) for s, c in zip(self.sems, self.cnt) if c > 0]


def build_nc():
    nc = bass.Bass("TRN2", target_bir_lowering=False)

    x_own = nc.dram_tensor("x_own", [TOWN, H], F32, kind="ExternalInput")
    wq = nc.dram_tensor("wq", [H, H], F32, kind="ExternalInput")
    wk = nc.dram_tensor("wk", [H, DKV], F32, kind="ExternalInput")
    wv = nc.dram_tensor("wv", [H, DKV], F32, kind="ExternalInput")
    wo = nc.dram_tensor("wo", [H, H], F32, kind="ExternalInput")
    wgT = nc.dram_tensor("wgT", [H, INTER], I32, kind="ExternalInput")
    wuT = nc.dram_tensor("wuT", [H, INTER], I32, kind="ExternalInput")
    wdT = nc.dram_tensor("wdT", [INTER, H], I32, kind="ExternalInput")
    sg_in = nc.dram_tensor("sg_in", [128, NIT], F32, kind="ExternalInput")
    su_in = nc.dram_tensor("su_in", [128, NIT], F32, kind="ExternalInput")
    sd_in = nc.dram_tensor("sd_in", [128, 32], F32, kind="ExternalInput")
    ropeQc = nc.dram_tensor("ropeQc", [128, TOWN], F32, kind="ExternalInput")
    ropeQs = nc.dram_tensor("ropeQs", [128, TOWN], F32, kind="ExternalInput")
    maskA = nc.dram_tensor("maskA", [128, 2432], BF16, kind="ExternalInput")
    maskB = nc.dram_tensor("maskB", [128, 4480], BF16, kind="ExternalInput")
    pswap = nc.dram_tensor("pswap", [128, 128], BF16, kind="ExternalInput")
    onesc = nc.dram_tensor("onesc", [1, 128], BF16, kind="ExternalInput")
    onesk = nc.dram_tensor("onesk", [128, 1], BF16, kind="ExternalInput")
    out = nc.dram_tensor("out", [TOWN, H], F32, kind="ExternalOutput")

    xno_s = nc.dram_tensor("xno_s", [TOWN, H], BF16, kind="Internal")
    qT_s = nc.dram_tensor("qT_s", [128, 32 * 1024], BF16, kind="Internal")
    ag_in = nc.dram_tensor("ag_in", [2 * TOWN, 1024], BF16, kind="Internal")
    ag_out = nc.dram_tensor("ag_out", [8 * TOWN, 1024], BF16, kind="Internal")
    yaT_s = nc.dram_tensor("yaT_s", [H, TOWN], BF16, kind="Internal")
    yfT_s = nc.dram_tensor("yfT_s", [H, TOWN], BF16, kind="Internal")
    h_s = nc.dram_tensor("h_s", [INTER, TOWN], BF16, kind="Internal")

    stack = ExitStack()
    sem = lambda n: stack.enter_context(nc.semaphore(n))
    sb = lambda n, shp, dt: stack.enter_context(nc.sbuf_tensor(n, shp, dt))
    ps = lambda n, shp: stack.enter_context(nc.psum_tensor(n, shp, F32))

    pswap_sb = sb("pswap_sb", [128, 128], BF16)
    onesc_sb = sb("onesc_sb", [1, 128], BF16)
    onesk_sb = sb("onesk_sb", [128, 1], BF16)
    negm_sb = sb("negm_sb", [128, 1], F32)
    sg_sb = sb("sg_sb", [128, NIT], F32)
    su_sb = sb("su_sb", [128, NIT], F32)
    sd_sb = sb("sd_sb", [128, 32], F32)
    rqc_sb = sb("rqc_sb", [128, TOWN], F32)
    rqs_sb = sb("rqs_sb", [128, TOWN], F32)

    r = Rec()
    s0 = sem("s0")
    n0 = 0
    for dst, src in [(pswap_sb, pswap), (onesc_sb, onesc), (onesk_sb, onesk),
                     (sg_sb, sg_in), (su_sb, su_in), (sd_sb, sd_in),
                     (rqc_sb, ropeQc), (rqs_sb, ropeQs)]:
        r.op("sync", "dma_start", dst[:, :], src.ap(), inc=s0, incv=16)
        n0 += 16
    r.op("vector", "memset", negm_sb[:, :], -EXP_SHIFT)
    r.op("sync", None, waits=[(s0, n0)])
    with nc.Block() as blk:
        r.replay_into(blk)

    with ExitStack() as ph:
        psb = lambda n, shp, dt: ph.enter_context(nc.sbuf_tensor(n, shp, dt))
        xb = [psb(f"p1xb{i}", [128, H], BF16) for i in range(2)]
        xnb = [psb(f"p1xnb{i}", [128, H], BF16) for i in range(2)]
        junk = psb("p1junk", [128, H], BF16)
        rr = [psb(f"p1rr{i}", [128, 1], F32) for i in range(2)]
        mneg = [psb(f"p1mneg{i}", [128, 1], F32) for i in range(2)]
        msq = [psb(f"p1msq{i}", [128, 1], F32) for i in range(2)]
        ssq = [psb(f"p1ssq{i}", [128, 1], F32) for i in range(2)]
        varb = [psb(f"p1var{i}", [128, 1], F32) for i in range(2)]
        tln = [psb(f"p1tln{i}", [128, 1], F32) for i in range(2)]
        rstd = [psb(f"p1rstd{i}", [128, 1], F32) for i in range(2)]
        lnb = [psb(f"p1lnb{i}", [128, 1], F32) for i in range(2)]

        sV, sSC, sV2, sXN = (sem(f"p1_{i}") for i in range(4))
        rLD = Ring(sem, "p1ld", 2)
        rST = Ring(sem, "p1st", 2)
        r = Rec()
        tiles = [(x_own, xno_s, i) for i in range(8)]
        for i, (src, dst, ti) in enumerate(tiles):
            p = i % 2
            w = [(sXN, i - 1)] if i >= 2 else []
            ldh, ldv = rLD.tick(p)
            r.op("gpsimd", "dma_start", xb[p][:, :], src.ap()[ti * 128:(ti + 1) * 128, :],
                 waits=w, inc=ldh, incv=16)
            r.op("vector", "reduce_sum", rr[p][:, :], xb[p][:, :], AX.X, negate=True,
                 waits=[(ldh, ldv)])
            r.op("vector", "drain")
            r.op("vector", "tensor_scalar_mul", mneg[p][:, :], rr[p][:, :], 1.0 / H)
            r.op("vector", "drain")
            r.op("vector", "tensor_mul", msq[p][:, :], mneg[p][:, :], mneg[p][:, :])
            r.op("vector", "scalar_tensor_tensor", junk[:, :], xb[p][:, :], 1.0, xb[p][:, :],
                 ALU.mult, ALU.mult, accum_out=ssq[p][:, :])
            r.op("vector", "drain")
            r.op("vector", "tensor_scalar_sub", msq[p][:, :], msq[p][:, :], 1e-5)
            r.op("vector", "drain")
            r.op("vector", "tensor_scalar", varb[p][:, :], ssq[p][:, :], 1.0 / H,
                 msq[p][:, :], ALU.mult, ALU.subtract, inc=sV)
            r.op("scalar", "activation", tln[p][:, :], varb[p][:, :], AF.Ln, bias=0.0,
                 scale=1.0, waits=[(sV, i + 1)])
            r.op("scalar", "drain")
            r.op("scalar", "activation", rstd[p][:, :], tln[p][:, :], AF.Exp, bias=0.0,
                 scale=-0.5, inc=sSC)
            r.op("vector", "tensor_mul", lnb[p][:, :], mneg[p][:, :], rstd[p][:, :],
                 waits=[(sSC, i + 1)], inc=sV2)
            r.op("vector", "drain")
            w = [rST.val(p)] if i >= 2 else []
            r.op("vector", "tensor_scalar", xnb[p][:, :], xb[p][:, :], rstd[p][:, :],
                 lnb[p][:, :], ALU.mult, ALU.add, waits=w, inc=sXN)
            sth, stv = rST.tick(p)
            r.op("scalar", "dma_start", dst.ap()[ti * 128:(ti + 1) * 128, :], xnb[p][:, :],
                 waits=[(sXN, i + 1)], inc=sth, incv=16)
        r.op("scalar", None, waits=rST.finals())
        with nc.Block() as blk:
            r.replay_into(blk)

    sAG = sem("sAG")
    with ExitStack() as ph:
        psb = lambda n, shp, dt: ph.enter_context(nc.sbuf_tensor(n, shp, dt))
        pps = lambda n, shp: ph.enter_context(nc.psum_tensor(n, shp, F32))
        wk_sb = psb("wk_sb", [128, NHB * 1024], BF16)
        wv_sb = psb("wv_sb", [128, NHB * 1024], BF16)
        xnT2 = psb("p2xnT", [128, NHB * 512], BF16)
        krot = psb("p2krot", [128, 8 * 512], BF16)
        kraw = [psb(f"p2kraw{i}", [128, 512], BF16) for i in range(2)]
        t1b = [psb(f"p2t1{i}", [128, 512], BF16) for i in range(2)]
        t2b = [psb(f"p2t2{i}", [128, 512], BF16) for i in range(2)]
        vbuf = psb("p2vbuf", [128, 4096], BF16)
        kps = [pps(f"p2kps{i}", [128, 512]) for i in range(2)]
        kpps = [pps(f"p2kpps{i}", [128, 512]) for i in range(2)]
        vps = [pps(f"p2vps{i}", [128, 512]) for i in range(2)]

        sWK, sWV, sKP, sKR, sSW, sT1, sROT, sVP, sVE = (sem(f"p2_{i}") for i in range(9))
        rXT = Ring(sem, "p2xt", 8)
        rSPK = Ring(sem, "p2spk", 1)
        rSPV = Ring(sem, "p2spv", 1)
        r = Rec()
        for hb in range(NHB):
            r.op("gpsimd", "dma_start", wk_sb[:, hb * 1024:(hb + 1) * 1024],
                 wk.ap()[hb * 128:(hb + 1) * 128, :], inc=sWK, incv=16)
        for hb in range(NHB):
            r.op("gpsimd", "dma_start", wv_sb[:, hb * 1024:(hb + 1) * 1024],
                 wv.ap()[hb * 128:(hb + 1) * 128, :], inc=sWV, incv=16)
        for g in range(2):
            w = [(sKP, 8 * g), (sVP, 8 * g)] if g >= 1 else []
            xt_w = {}
            for hb in range(NHB):
                xth, xtv = rXT.tick(hb % 8)
                xt_w[hb] = (xth, xtv)
                r.op("sync", "dma_start",
                     xnT2[:, hb * 512:(hb + 1) * 512],
                     xno_s.ap()[g * 512:(g + 1) * 512, hb * 128:(hb + 1) * 128],
                     transpose=True, waits=(w if hb == 0 else ()), inc=xth, incv=16)
            pend_swap = []

            def flush_swap():
                idx2, dp2 = pend_swap.pop(0)
                w2 = [(sKR, idx2 + 1)]
                if idx2 >= 2:
                    w2.append((sROT, idx2 - 1))
                r.op("tensor", "matmul", kpps[dp2][:, :], pswap_sb[:, :], kraw[dp2][:, :],
                     start=True, stop=True, waits=w2, inc=sSW)

            for dkb in range(8):
                idx = g * 8 + dkb
                dp = idx % 2
                w = [(sWK, 16 * NHB)] if idx == 0 else []
                if idx >= 2:
                    w.append((sKR, idx - 1))
                for hb in range(NHB):
                    wq_ = list(w) if hb == 0 else []
                    if dkb == 0:
                        wq_.append(xt_w[hb])
                    r.op("tensor", "matmul", kps[dp][:, :],
                         wk_sb[:, hb * 1024 + dkb * 128: hb * 1024 + (dkb + 1) * 128],
                         xnT2[:, hb * 512:(hb + 1) * 512],
                         start=(hb == 0), stop=(hb == NHB - 1),
                         waits=(wq_ if (hb == 0 or dkb == 0) else ()),
                         inc=(sKP if hb == NHB - 1 else None))
                w = [(sKP, idx + 1)]
                if idx >= 2:
                    w += [(sSW, idx - 1), (sT1, idx - 1)]
                r.op("scalar", "copy", kraw[dp][:, :], kps[dp][:, :], waits=w, inc=sKR)
                pend_swap.append((idx, dp))
                if len(pend_swap) > 1:
                    flush_swap()
                w = [(sKR, idx + 1)]
                if g >= 1 and dkb == 0:
                    w.append(rSPK.val(0))
                r.op("vector", "tensor_mul", t1b[dp][:, :], kraw[dp][:, :],
                     rqc_sb[:, g * 512:(g + 1) * 512], waits=w, inc=sT1)
                r.op("vector", "tensor_mul", t2b[dp][:, :], kpps[dp][:, :],
                     rqs_sb[:, g * 512:(g + 1) * 512], waits=[(sSW, idx + 1)])
                r.op("vector", "drain")
                r.op("vector", "tensor_add", krot[:, dkb * 512:(dkb + 1) * 512],
                     t1b[dp][:, :], t2b[dp][:, :], inc=sROT)
            while pend_swap:
                flush_swap()
            dst = bass.AP(ag_in, g * 512, [[TOWN, 128], [128 * TOWN, 8], [1, 512]])
            src = bass.AP(krot, 0, [[8 * 512, 128], [512, 8], [1, 512]])
            sph, spv = rSPK.tick(0)
            r.op("scalar", "dma_start", dst, src, waits=[(sROT, 8 * (g + 1))], inc=sph, incv=16)
            for tc in range(4):
                for dvb in range(2):
                    vidx = (g * 4 + tc) * 2 + dvb
                    vp = vidx % 2
                    w = []
                    if vidx == 0:
                        w.append((sWV, 16 * NHB))
                    if vidx >= 2:
                        w.append((sVE, vidx - 1))
                    for hb in range(NHB):
                        r.op("tensor", "matmul", vps[vp][:, :],
                             xnT2[:, hb * 512 + tc * 128: hb * 512 + (tc + 1) * 128],
                             wv_sb[:, hb * 1024 + dvb * 512: hb * 1024 + (dvb + 1) * 512],
                             start=(hb == 0), stop=(hb == NHB - 1),
                             waits=(w if hb == 0 else ()),
                             inc=(sVP if hb == NHB - 1 else None))
                    w = [(sVP, vidx + 1)]
                    if g >= 1 and tc == 0 and dvb == 0:
                        w.append(rSPV.val(0))
                    r.op("scalar", "copy", vbuf[:, tc * 1024 + dvb * 512: tc * 1024 + (dvb + 1) * 512],
                         vps[vp][:, :], waits=w, inc=sVE)
            dst = bass.AP(ag_in, (TOWN + g * 512) * 1024, [[1024, 128], [128 * 1024, 4], [1, 1024]])
            src = bass.AP(vbuf, 0, [[4096, 128], [1024, 4], [1, 1024]])
            sph, spv = rSPV.tick(0)
            r.op("scalar", "dma_start", dst, src, waits=[(sVE, 8 * (g + 1))], inc=sph, incv=16)
        r.op("gpsimd", "collective_compute", "AllGather", ALU.bypass,
             [[0, 1, 2, 3], [4, 5, 6, 7]],
             [ag_in.ap().opt()], [ag_out.ap().opt()],
             waits=[rSPK.val(0), rSPV.val(0)], inc=sAG, incv=1)
        with nc.Block() as blk:
            r.replay_into(blk)

    wgwu_scope = ExitStack()
    wg_sb = [wgwu_scope.enter_context(nc.sbuf_tensor(f"wg_sb{i}", [128, NHB * 128], BF16)) for i in range(2)]
    wu_sb = [wgwu_scope.enter_context(nc.sbuf_tensor(f"wu_sb{i}", [128, NHB * 128], BF16)) for i in range(2)]
    rWG = Ring(sem, "p4wg", 2)
    with ExitStack() as ph:
        psb = lambda n, shp, dt: ph.enter_context(nc.sbuf_tensor(n, shp, dt))
        qT_sb = psb("qT_sb", [128, 32 * 1024], BF16)

        with ExitStack() as pa:
            asb = lambda n, shp, dt: pa.enter_context(nc.sbuf_tensor(n, shp, dt))
            aps = lambda n, shp: pa.enter_context(nc.psum_tensor(n, shp, F32))
            xnTq = asb("xnTq", [128, NHB * 1024], BF16)
            wq_sb = [asb(f"wq_sb{i}", [128, NHB * 128], BF16) for i in range(2)]
            qraw = [asb(f"qraw{i}", [128, 1024], BF16) for i in range(2)]
            t1qb = [asb(f"t1q{i}", [128, 1024], BF16) for i in range(2)]
            t2qb = [asb(f"t2q{i}", [128, 1024], BF16) for i in range(2)]
            qps = [aps(f"qps{i}", [128, 1024]) for i in range(2)]
            kpq = [aps(f"kpq{i}", [128, 1024]) for i in range(2)]

            sQP, sQE, sQSW, sQT1, sQR = (sem(f"p3a_{i}") for i in range(5))
            rWQ = Ring(sem, "p3awq", 2)
            rXQ = Ring(sem, "p3axq", 8)
            xq_w = {}
            r = Rec()
            for hb in range(NHB):
                xqh, xqv = rXQ.tick(hb % 8)
                xq_w[hb] = (xqh, xqv)
                r.op("sync", "dma_start",
                     xnTq[:, hb * 1024:(hb + 1) * 1024],
                     xno_s.ap()[0:TOWN, hb * 128:(hb + 1) * 128],
                     transpose=True, inc=xqh, incv=16)
            for dqb in range(32):
                p = dqb % 2
                w = [(sQP, dqb - 1)] if dqb >= 2 else []
                src = bass.AP(wq, dqb * 128, [[H, 128], [128 * H, NHB], [1, 128]])
                dst = bass.AP(wq_sb[p], 0, [[NHB * 128, 128], [128, NHB], [1, 128]])
                wqh, wqv = rWQ.tick(p)
                r.op("gpsimd", "dma_start", dst, src, waits=w, inc=wqh, incv=16)
                w = [(wqh, wqv)]
                if dqb >= 2:
                    w.append((sQR, dqb - 1))
                for hb in range(NHB):
                    wq_ = list(w) if hb == 0 else []
                    if dqb == 0:
                        wq_.append(xq_w[hb])
                    for h5 in range(2):
                        r.op("tensor", "matmul", qps[p][:, h5 * 512:(h5 + 1) * 512],
                             wq_sb[p][:, hb * 128:(hb + 1) * 128],
                             xnTq[:, hb * 1024 + h5 * 512: hb * 1024 + (h5 + 1) * 512],
                             start=(hb == 0), stop=(hb == NHB - 1),
                             waits=(wq_ if h5 == 0 else ()),
                             inc=(sQP if (hb == NHB - 1 and h5 == 1) else None))
                w = [(sQP, dqb + 1)]
                if dqb >= 2:
                    w += [(sQSW, dqb - 1), (sQT1, dqb - 1)]
                r.op("scalar", "copy", qraw[p][:, :], qps[p][:, :], waits=w, inc=sQE)
                w = [(sQE, dqb + 1)]
                if dqb >= 2:
                    w.append((sQR, dqb - 1))
                for h5 in range(2):
                    r.op("tensor", "matmul", kpq[p][:, h5 * 512:(h5 + 1) * 512],
                         pswap_sb[:, :], qraw[p][:, h5 * 512:(h5 + 1) * 512],
                         start=True, stop=True, waits=(w if h5 == 0 else ()),
                         inc=(sQSW if h5 == 1 else None))
                r.op("vector", "tensor_mul", t1qb[p][:, :], qraw[p][:, :], rqc_sb[:, :],
                     waits=[(sQE, dqb + 1)], inc=sQT1)
                r.op("vector", "tensor_mul", t2qb[p][:, :], kpq[p][:, :], rqs_sb[:, :],
                     waits=[(sQSW, dqb + 1)])
                r.op("vector", "drain")
                r.op("vector", "tensor_add", qT_sb[:, dqb * 1024:(dqb + 1) * 1024],
                     t1qb[p][:, :], t2qb[p][:, :], inc=sQR)
            sQS = sem("p3a_qs")
            r.op("sync", "dma_start", qT_s.ap(), qT_sb[:, :],
                 waits=[(sQR, 32)], inc=sQS, incv=16)
            for it_pf in range(2):
                p_pf = it_pf % 2
                src = bass.AP(wgT, it_pf * 128, [[INTER, 128], [128 * INTER, NHB], [1, 128]])
                dstp = bass.AP(wg_sb[p_pf], 0, [[NHB * 128, 128], [128, NHB], [1, 128]])
                wgh, _ = rWG.tick(p_pf)
                r.op("gpsimd", "dma_start", dstp, src, inc=wgh, incv=16)
                src = bass.AP(wuT, it_pf * 128, [[INTER, 128], [128 * INTER, NHB], [1, 128]])
                dstp = bass.AP(wu_sb[p_pf], 0, [[NHB * 128, 128], [128, NHB], [1, 128]])
                wgh, _ = rWG.tick(p_pf)
                r.op("gpsimd", "dma_start", dstp, src, inc=wgh, incv=16)
            r.op("sync", None, waits=[(sQS, 16)])
            with nc.Block() as blk:
                r.replay_into(blk)


    NCACHE = 44
    ffn_scope = ExitStack()
    hcache = ffn_scope.enter_context(
        nc.sbuf_tensor("hcache", [128, NCACHE * 1024], BF16))

    with ExitStack() as ph:
        psb = lambda n, shp, dt: ph.enter_context(nc.sbuf_tensor(n, shp, dt))
        pps = lambda n, shp: ph.enter_context(nc.psum_tensor(n, shp, F32))
        xnTf = psb("xnTf", [128, NHB * 1024], BF16)
        sgb = [psb(f"sgb{i}", [128, 1024], BF16) for i in range(2)]
        hbuf = [psb(f"hbuf{i}", [128, 1024], BF16) for i in range(2)]
        gps = [pps(f"gps{i}", [128, 1024]) for i in range(2)]
        ups = [pps(f"ups{i}", [128, 1024]) for i in range(2)]

        sGU, sSL, sHH = (sem(f"p4_{i}") for i in range(3))
        rHS = Ring(sem, "p4hs", 2)
        rXF = Ring(sem, "p4xf", 8)
        xf_w = {}
        r = Rec()
        for hb in range(NHB):
            xfh, xfv = rXF.tick(hb % 8)
            xf_w[hb] = (xfh, xfv)
            r.op("sync", "dma_start",
                 xnTf[:, hb * 1024:(hb + 1) * 1024],
                 xno_s.ap()[0:TOWN, hb * 128:(hb + 1) * 128], transpose=True,
                 inc=xfh, incv=16)
        for it in range(NIT):
            p = it % 2
            if it >= 2:
                w = [(sGU, it - 1)]
                src = bass.AP(wgT, it * 128, [[INTER, 128], [128 * INTER, NHB], [1, 128]])
                dst = bass.AP(wg_sb[p], 0, [[NHB * 128, 128], [128, NHB], [1, 128]])
                wgh, _ = rWG.tick(p)
                r.op("gpsimd", "dma_start", dst, src, waits=w, inc=wgh, incv=16)
                src = bass.AP(wuT, it * 128, [[INTER, 128], [128 * INTER, NHB], [1, 128]])
                dst = bass.AP(wu_sb[p], 0, [[NHB * 128, 128], [128, NHB], [1, 128]])
                wgh, wgv = rWG.tick(p)
                r.op("gpsimd", "dma_start", dst, src, inc=wgh, incv=16)
            else:
                wgh, wgv = rWG.val(p)
            w = [(wgh, wgv)]
            if it >= 2:
                w += [(sSL, it - 1), (sHH, it - 1)]
            for hb in range(NHB):
                wq_ = list(w) if hb == 0 else []
                if it == 0:
                    wq_.append(xf_w[hb])
                for h5 in range(2):
                    r.op("tensor", "matmul", gps[p][:, h5 * 512:(h5 + 1) * 512],
                         wg_sb[p][:, hb * 128:(hb + 1) * 128],
                         xnTf[:, hb * 1024 + h5 * 512: hb * 1024 + (h5 + 1) * 512],
                         start=(hb == 0), stop=(hb == NHB - 1),
                         waits=(wq_ if h5 == 0 else ()))
                for h5 in range(2):
                    r.op("tensor", "matmul", ups[p][:, h5 * 512:(h5 + 1) * 512],
                         wu_sb[p][:, hb * 128:(hb + 1) * 128],
                         xnTf[:, hb * 1024 + h5 * 512: hb * 1024 + (h5 + 1) * 512],
                         start=(hb == 0), stop=(hb == NHB - 1),
                         inc=(sGU if (hb == NHB - 1 and h5 == 1) else None))
            w = [(sGU, it + 1)]
            if it >= 2:
                w.append((sHH, it - 1))
            r.op("scalar", "activation", sgb[p][:, :], gps[p][:, :], AF.Silu,
                 bias=0.0, scale=sg_sb[:, it:it + 1], waits=w, inc=sSL)
            w = [(sSL, it + 1)]
            if it < NCACHE:
                r.op("vector", "scalar_tensor_tensor",
                     hcache[:, it * 1024:(it + 1) * 1024], ups[p][:, :],
                     su_sb[:, it:it + 1], sgb[p][:, :], ALU.mult, ALU.mult,
                     waits=w, inc=sHH)
            else:
                if it >= NCACHE + 2:
                    w.append(rHS.val(p))
                r.op("vector", "scalar_tensor_tensor", hbuf[p][:, :], ups[p][:, :],
                     su_sb[:, it:it + 1], sgb[p][:, :], ALU.mult, ALU.mult,
                     waits=w, inc=sHH)
                hsh, hsv = rHS.tick(p)
                r.op("sync", "dma_start", h_s.ap()[it * 128:(it + 1) * 128, :],
                     hbuf[p][:, :], waits=[(sHH, it + 1)], inc=hsh, incv=16)
        r.op("sync", None, waits=rHS.finals())
        with nc.Block() as blk:
            r.replay_into(blk)

    sQL = sem("p3b_ql")

    with ExitStack() as ph:
        psb = lambda n, shp, dt: ph.enter_context(nc.sbuf_tensor(n, shp, dt))
        pps = lambda n, shp: ph.enter_context(nc.psum_tensor(n, shp, F32))
        hst = [psb(f"hst{i}", [128, 2048], BF16) for i in range(4)]
        wd_sb = [psb(f"wd_sb{i}", [128, 1024], BF16) for i in range(4)]
        yf_sb = [psb(f"yf_sb{i}", [128, 1024], BF16) for i in range(2)]
        yfps = [pps(f"yfps{i}", [128, 1024]) for i in range(4)]

        sDM, sYF = (sem(f"p5_{i}") for i in range(2))
        rHL = Ring(sem, "p5hl", 4)
        rWD = Ring(sem, "p5wd", 4)
        rYS = Ring(sem, "p5ys", 2)
        NP2 = NIT // 2
        NCP2 = NCACHE // 2
        hl_last = {}
        stream_idx = 0
        r = Rec()
        for db in range(8):
            for it2 in range(NP2):
                it0 = 2 * it2
                lp = db * NP2 + it2
                cached = it2 < NCP2
                w = []
                if cached:
                    hbase = it0 * 1024
                    htens = hcache
                else:
                    slot = stream_idx % 4
                    stream_idx += 1
                    if slot in hl_last:
                        w = [(sDM, hl_last[slot] + 1)]
                    hl_last[slot] = lp
                    hlh, hlv = rHL.tick(slot)
                    src = bass.AP(h_s, it0 * 128 * TOWN,
                                  [[TOWN, 128], [128 * TOWN, 2], [1, TOWN]])
                    dst = bass.AP(hst[slot], 0, [[2048, 128], [1024, 2], [1, 1024]])
                    r.op("sync", "dma_start", dst, src, waits=w, inc=hlh, incv=16)
                    hbase = 0
                    htens = hst[slot]
                w = [(sDM, lp - 2)] if lp >= 3 else []
                src = bass.AP(wdT, it0 * 128 * H + db * 512,
                              [[H, 128], [128 * H, 2], [1, 512]])
                dst = bass.AP(wd_sb[lp % 4], 0, [[1024, 128], [512, 2], [1, 512]])
                wdh, wdv = rWD.tick(lp % 4)
                r.op("gpsimd", "dma_start", dst, src, waits=w, inc=wdh, incv=16)
                w = [(wdh, wdv)]
                if not cached:
                    w.append((hlh, hlv))
                if it2 == 0 and db >= 1:
                    w.append((sYF, 4 * db))
                for itl in range(2):
                    for dt in range(4):
                        for h5 in range(2):
                            r.op("tensor", "matmul", yfps[dt][:, h5 * 512:(h5 + 1) * 512],
                                 wd_sb[lp % 4][:, itl * 512 + dt * 128: itl * 512 + (dt + 1) * 128],
                                 htens[:, hbase + itl * 1024 + h5 * 512: hbase + itl * 1024 + (h5 + 1) * 512],
                                 start=(it2 == 0 and itl == 0), stop=(it2 == NP2 - 1 and itl == 1),
                                 waits=(w if (itl == 0 and dt == 0 and h5 == 0) else ()),
                                 inc=(sDM if (itl == 1 and dt == 3 and h5 == 1) else None))
            for dt in range(4):
                di = db * 4 + dt
                w = [(sDM, (db + 1) * NP2)]
                if di >= 2:
                    w.append(rYS.val(dt % 2))
                r.op("scalar", "activation", yf_sb[dt % 2][:, :], yfps[dt][:, :],
                     AF.Copy, bias=0.0, scale=sd_sb[:, di:di + 1], waits=w, inc=sYF)
                ysh, ysv = rYS.tick(dt % 2)
                r.op("scalar", "dma_start", yfT_s.ap()[di * 128:(di + 1) * 128, :],
                     yf_sb[dt % 2][:, :], waits=[(sYF, di + 1)], inc=ysh, incv=16)
        r.op("scalar", None, waits=rYS.finals())
        with nc.Block() as blk:
            r.replay_into(blk)
    ffn_scope.close()
    wgwu_scope.close()
    with ExitStack() as ph:
        psb = lambda n, shp, dt: ph.enter_context(nc.sbuf_tensor(n, shp, dt))
        qT_sb = psb("qT_attn", [128, 32 * 1024], BF16)
        attnT = psb("attnT", [128, 32 * 1024], BF16)
        r = Rec()
        r.op("sync", "dma_start", qT_sb[:, :], qT_s.ap(), inc=sQL, incv=16)
        with nc.Block() as blk:
            r.replay_into(blk)
        with ExitStack() as pb:
            bsb = lambda n, shp, dt: pb.enter_context(nc.sbuf_tensor(n, shp, dt))
            bps = lambda n, shp: pb.enter_context(nc.psum_tensor(n, shp, F32))
            kTb = [bsb(f"kTb{i}", [128, S], BF16) for i in range(2)]
            vTb = [bsb(f"vTb{i}", [128, S], BF16) for i in range(2)]
            mA_sb = bsb("mA_sb", [128, 896], BF16)
            mB_sb = bsb("mB_sb", [128, 896], BF16)
            pt = [bsb(f"pt{i}", [128, 1024], BF16) for i in range(3)]
            ptm = [bsb(f"ptm{i}", [128, 1024], BF16) for i in range(3)]
            o_sb = [bsb(f"o_sb{i}", [128, 512], BF16) for i in range(2)]
            accb = [bsb(f"accb{i}", [128, 1024], BF16) for i in range(2)]
            accbf = [bsb(f"accbf{i}", [128, 512], BF16) for i in range(2)]
            rs_sb = [bsb(f"rs_sb{i}", [1, 512], BF16) for i in range(2)]
            rs_f = [bsb(f"rs_f{i}", [1, 512], F32) for i in range(2)]
            Sps = [bps(f"Sps{i}", [128, 1024]) for i in range(2)]
            Ops = [bps(f"Ops{i}", [128, 512]) for i in range(2)]
            s1ps = bps("s1ps", [1, 512])
            sbcps = bps("sbcps", [128, 512])

            sS_, sEXP, sMSK, sAV, sRC, sBC, sOE, sATT, sMS0, sACC, sACB, sS1 = (
                sem(f"p3b_{i}") for i in range(12))
            rKH = Ring(sem, "p3bkh", 2)


            def gt_base(gt):
                rnk, rem = divmod(gt, 8)
                half, sub = divmod(rem, 4)
                chunk = rnk if half == 0 else 7 - rnk
                return chunk * 512 + sub * 128

            def chunk_tiles(c):
                rnk = c if c <= 3 else 7 - c
                half = 0 if c <= 3 else 1
                return [rnk * 8 + half * 4 + sub for sub in range(4)]

            def arm_qc_tiles(j):
                WA0 = 1536 - j * 512
                WB0 = 3968 - (7 - j) * 512 - 384
                A, B = [], []
                for c in range(j + 1):
                    diag = (c == j)
                    for gt in chunk_tiles(c):
                        A.append((gt, diag, 0, (1920 - gt_base(gt) - WA0) if diag else None))
                for c in range(8 - j):
                    diag = (c == 7 - j)
                    for gt in chunk_tiles(c):
                        B.append((gt, diag, 1, (3968 - gt_base(gt) - WB0) if diag else None))
                return {0: A, 1: B}, WA0, WB0

            def build_arm(j):
                qc_tiles, WA0, WB0 = arm_qc_tiles(j)
                items = []
                units = []
                kh_load = {}
                mcount = 0
                for kh in range(8):
                    for qc in range(2):
                        tl = qc_tiles[qc]
                        nk2 = len(tl) // 2
                        for qh in range(4):
                            uidx = len(units)
                            first_i = len(items)
                            for k2 in range(nk2):
                                i = len(items)
                                pair = (tl[2 * k2], tl[2 * k2 + 1])
                                masked = pair[0][1]
                                assert pair[1][1] == masked
                                if masked:
                                    mcount += 1
                                items.append(dict(
                                    i=i, kh=kh, kp=kh % 2, qc=qc, qh=qh,
                                    qhg=kh * 4 + qh, qoff=qc * 512,
                                    k2=k2, pair=pair,
                                    first=(k2 == 0), last=(k2 == nk2 - 1),
                                    masked=masked, mval=mcount, unit=uidx))
                            units.append(dict(u=uidx, kh=kh, qhg=kh * 4 + qh,
                                              qoff=qc * 512, first_i=first_i,
                                              last_i=len(items) - 1))
                n_items = len(items)
                masked_items = [it for it in items if it["masked"]]

                r = Rec()
                r.op("sync", "dma_start", mA_sb[:, :], maskA.ap()[:, WA0:WA0 + 896],
                     inc=sMS0, incv=16)
                r.op("sync", "dma_start", mB_sb[:, :], maskB.ap()[:, WB0:WB0 + 896],
                     inc=sMS0, incv=16)

                def pt_guard(jj):
                    it = items[jj]
                    if it["masked"]:
                        return [(sMSK, it["mval"])]
                    return [(sAV, jj + 1), (sACC, jj + 1)]

                for kh in range(8):
                    kp = kh % 2
                    w = [(sATT, 8 * (kh - 1))] if kh >= 2 else []
                    if kh == 0:
                        w = w + [(sAG, 1)]
                    khh, _ = rKH.tick(kp)
                    src = bass.AP(ag_out, kh * 128 * 1024,
                                  [[1024, 128], [2 * TOWN * 1024, 4], [1, 1024]])
                    dst = bass.AP(kTb[kp], 0, [[S, 128], [1024, 4], [1, 1024]])
                    r.op("sync", "dma_start", dst, src, waits=w, inc=khh, incv=16)
                    for rk in range(4):
                        khh, khv = rKH.tick(kp)
                        src = bass.AP(ag_out, (rk * 2 * TOWN + TOWN) * 1024 + kh * 128,
                                      [[1024, 128], [128 * 1024, 8], [1, 128]])
                        dst = bass.AP(vTb[kp], rk * 1024, [[S, 128], [128, 8], [1, 128]])
                        r.op("sync", "dma_start", dst, src, inc=khh, incv=16)
                    kh_load[kh] = (khh, khv)

                for it in items:
                    i = it["i"]
                    w = [(sS_, i + 1)]
                    if i >= 3:
                        w.extend(pt_guard(i - 3))
                    r.op("scalar", "activation", pt[i % 3][:, :], Sps[i % 2][:, :],
                         AF.Exp, bias=negm_sb[:, :], scale=ISQ, waits=w, inc=sEXP)
                    if it["last"]:
                        uu = it["unit"]
                        w = [(sAV, i + 1)]
                        if uu >= 2:
                            w.append((sATT, uu - 1))
                        r.op("scalar", "copy", o_sb[uu % 2][:, :], Ops[uu % 2][:, :],
                             waits=w, inc=sOE)

                for it in items:
                    i = it["i"]
                    uu = it["unit"]
                    if it["masked"]:
                        mv = it["mval"]
                        w = [(sEXP, i + 1)]
                        if mv == 1:
                            w.append((sMS0, 32))
                        if mv >= 4:
                            pj = masked_items[mv - 4]["i"]
                            w += [(sAV, pj + 1), (sACC, pj + 1)]
                        for i_kt, (gt, _m, mid, moff) in enumerate(it["pair"]):
                            msk = mA_sb if mid == 0 else mB_sb
                            r.op("vector", "tensor_mul",
                                 ptm[(mv - 1) % 3][:, i_kt * 512:(i_kt + 1) * 512],
                                 pt[i % 3][:, i_kt * 512:(i_kt + 1) * 512],
                                 msk[:, moff:moff + 512],
                                 waits=(w if i_kt == 0 else ()),
                                 inc=(sMSK if i_kt == 1 else None))
                        src_t = ptm[(mv - 1) % 3]
                        wadd = []
                    else:
                        src_t = pt[i % 3]
                        wadd = [(sEXP, i + 1)]
                    m = it["k2"]
                    if m == 0:
                        r.op("vector", "tensor_copy", accb[1][:, :], src_t[:, :],
                             waits=wadd, inc=sACC)
                    else:
                        r.op("vector", "tensor_add", accb[(m + 1) % 2][:, :],
                             accb[m % 2][:, :], src_t[:, :],
                             waits=wadd, inc=sACC)
                    if it["last"]:
                        nfin = (m + 1) % 2
                        w = [(sS1, uu - 1)] if uu >= 2 else []
                        r.op("vector", "tensor_add", accbf[uu % 2][:, :],
                             accb[nfin][:, 0:512], accb[nfin][:, 512:1024],
                             waits=w, inc=sACB)
                        w = [(sS1, uu + 1)]
                        if uu >= 2:
                            w.append((sBC, uu - 1))
                        r.op("vector", "reciprocal", rs_f[uu % 2][:, :], s1ps[:, :], waits=w)
                        r.op("vector", "drain")
                        r.op("vector", "tensor_copy", rs_sb[uu % 2][:, :], rs_f[uu % 2][:, :],
                             inc=sRC)
                        un = units[uu]
                        r.op("vector", "tensor_mul",
                             attnT[:, un["qhg"] * 1024 + un["qoff"]: un["qhg"] * 1024 + un["qoff"] + 512],
                             o_sb[uu % 2][:, :], sbcps[:, :],
                             waits=[(sOE, uu + 1), (sBC, uu + 1)], inc=sATT)

                LAG = 2

                def emit_S(it):
                    i = it["i"]
                    w = [(sEXP, i - 1)] if i >= 2 else []
                    if it["first"] and it["qc"] == 0 and it["qh"] == 0:
                        w = w + [kh_load[it["kh"]]]
                        if it["kh"] == 0:
                            w = w + [(sQL, 16)]
                    for i_kt, (gt, _m, _mid, _mo) in enumerate(it["pair"]):
                        r.op("tensor", "matmul", Sps[i % 2][:, i_kt * 512:(i_kt + 1) * 512],
                             kTb[it["kp"]][:, gt * 128:(gt + 1) * 128],
                             qT_sb[:, it["qhg"] * 1024 + it["qoff"]: it["qhg"] * 1024 + it["qoff"] + 512],
                             start=True, stop=True,
                             waits=(w if i_kt == 0 else ()),
                             inc=(sS_ if i_kt == 1 else None))

                def emit_AV(it):
                    i = it["i"]
                    uu = it["unit"]
                    if it["masked"]:
                        w = [(sMSK, it["mval"])]
                        src_t = ptm[(it["mval"] - 1) % 3]
                    else:
                        w = [(sEXP, i + 1)]
                        src_t = pt[i % 3]
                    if it["first"]:
                        if uu >= 2:
                            w.append((sOE, uu - 1))
                        if uu >= 1:
                            w.append((sRC, uu))
                    for i_kt, (gt, _m, _mid, _mo) in enumerate(it["pair"]):
                        r.op("tensor", "matmul", Ops[uu % 2][:, :],
                             vTb[it["kp"]][:, gt * 128:(gt + 1) * 128],
                             src_t[:, i_kt * 512:(i_kt + 1) * 512],
                             start=(it["first"] and i_kt == 0), stop=(it["last"] and i_kt == 1),
                             waits=(w if i_kt == 0 else ()),
                             inc=(sAV if i_kt == 1 else None))
                    if it["last"]:
                        w = [(sACB, uu + 1)]
                        if uu >= 1:
                            w.append((sRC, uu))
                        r.op("tensor", "matmul", s1ps[:, :], onesk_sb[:, :],
                             accbf[uu % 2][:, :], start=True, stop=True,
                             waits=w, inc=sS1)
                        w = [(sRC, uu + 1)]
                        if uu >= 1:
                            w.append((sATT, uu))
                        r.op("tensor", "matmul", sbcps[:, :], onesc_sb[:, :], rs_sb[uu % 2][:, :],
                             start=True, stop=True, waits=w, inc=sBC)

                for i in range(n_items + LAG):
                    if i < n_items:
                        emit_S(items[i])
                    jj = i - LAG
                    if jj >= 0:
                        emit_AV(items[jj])

                r.op("vector", None, waits=[(sATT, 64)])
                r.op("sync", None, waits=[(sATT, 64)])
                r.op("scalar", None, waits=[(sATT, 64)])
                return r, n_items, mcount

            sw_engs = [nc.sync, nc.tensor, nc.scalar, nc.vector]
            sw_idx = []
            for e in sw_engs:
                pid = e.partition_id()
                sw_idx.append(e.compute_val(pid % 4))
            ring_snap = list(rKH.cnt)
            totals = None
            for j in nc.Switch(engines=sw_engs, index=sw_idx, n=4):
                rKH.cnt = list(ring_snap)
                rarm, n_items, mcount = build_arm(j)
                if totals is None:
                    totals = (n_items, mcount, list(rKH.cnt))
                else:
                    assert totals == (n_items, mcount, list(rKH.cnt)), \
                        f"arm {j} semaphore totals diverge"
                for name in ENGINES:
                    engobj = getattr(nc, name)
                    for (m, args, kw, waits, inc, incv) in rarm.q[name]:
                        for (sh, v) in waits:
                            if v > 0:
                                engobj.wait_ge(sh, v)
                        if m is None:
                            continue
                        ins = getattr(engobj, m)(*args, **kw)
                        if inc is not None:
                            ins.then_inc(inc, incv)
            rKH.cnt = list(totals[2])

        with ExitStack() as pc:
            csb = lambda n, shp, dt: pc.enter_context(nc.sbuf_tensor(n, shp, dt))
            cps = lambda n, shp: pc.enter_context(nc.psum_tensor(n, shp, F32))
            wo_sb = [csb(f"wo_sb{i}", [128, NHB * 128], BF16) for i in range(2)]
            ya_sb = [csb(f"ya_sb{i}", [128, 1024], BF16) for i in range(2)]
            yaps = [cps(f"yaps{i}", [128, 1024]) for i in range(2)]

            sYA, sYAE = (sem(f"p3c_{i}") for i in range(2))
            rWO = Ring(sem, "p3cwo", 2)
            rYAS = Ring(sem, "p3cys", 2)
            r = Rec()
            for db in range(32):
                p = db % 2
                w = [(sYA, db - 1)] if db >= 2 else []
                src = bass.AP(wo, db * 128, [[H, 128], [128 * H, NHB], [1, 128]])
                dst = bass.AP(wo_sb[p], 0, [[NHB * 128, 128], [128, NHB], [1, 128]])
                woh, wov = rWO.tick(p)
                r.op("gpsimd", "dma_start", dst, src, waits=w, inc=woh, incv=16)
                w = [(woh, wov)]
                if db >= 2:
                    w.append((sYAE, db - 1))
                for cb in range(NHB):
                    for h5 in range(2):
                        r.op("tensor", "matmul", yaps[p][:, h5 * 512:(h5 + 1) * 512],
                             wo_sb[p][:, cb * 128:(cb + 1) * 128],
                             attnT[:, cb * 1024 + h5 * 512: cb * 1024 + (h5 + 1) * 512],
                             start=(cb == 0), stop=(cb == NHB - 1),
                             waits=(w if (cb == 0 and h5 == 0) else ()),
                             inc=(sYA if (cb == NHB - 1 and h5 == 1) else None))
                w = [(sYA, db + 1)]
                if db >= 2:
                    w.append(rYAS.val(p))
                r.op("scalar", "copy", ya_sb[p][:, :], yaps[p][:, :], waits=w, inc=sYAE)
                ysh, ysv = rYAS.tick(p)
                r.op("scalar", "dma_start", yaT_s.ap()[db * 128:(db + 1) * 128, :],
                     ya_sb[p][:, :], waits=[(sYAE, db + 1)], inc=ysh, incv=16)
            r.op("scalar", None, waits=rYAS.finals())
            with nc.Block() as blk:
                r.replay_into(blk)

    with ExitStack() as ph:
        psb = lambda n, shp, dt: ph.enter_context(nc.sbuf_tensor(n, shp, dt))
        xo = [psb(f"p6x{i}", [128, H], F32) for i in range(2)]
        ya = [psb(f"p6ya{i}", [128, H], BF16) for i in range(2)]
        yf = [psb(f"p6yf{i}", [128, H], BF16) for i in range(2)]
        t1b = [psb(f"p6t1{i}", [128, H], F32) for i in range(2)]
        ob = [psb(f"p6o{i}", [128, H], F32) for i in range(2)]

        sA2 = sem("p6_a2")
        rXL = Ring(sem, "p6xl", 2)
        rYL = Ring(sem, "p6yl", 2)
        rOS = Ring(sem, "p6os", 2)
        r = Rec()
        for tt in range(8):
            p = tt % 2
            w = [(sA2, tt - 1)] if tt >= 2 else []
            xlh, xlv = rXL.tick(p)
            r.op("sync", "dma_start", xo[p][:, :], x_own.ap()[tt * 128:(tt + 1) * 128, :],
                 waits=w, inc=xlh, incv=16)
            ylh, _ = rYL.tick(p)
            r.op("sync", "dma_start", ya[p][:, :],
                 yaT_s.ap()[0:H, tt * 128:(tt + 1) * 128], transpose=True,
                 inc=ylh, incv=16)
            ylh, ylv = rYL.tick(p)
            r.op("sync", "dma_start", yf[p][:, :],
                 yfT_s.ap()[0:H, tt * 128:(tt + 1) * 128], transpose=True,
                 inc=ylh, incv=16)
            w = [(xlh, xlv), (ylh, ylv)]
            if tt >= 2:
                w.append(rOS.val(p))
            r.op("vector", "tensor_add", t1b[p][:, :], xo[p][:, :], ya[p][:, :], waits=w)
            r.op("vector", "drain")
            r.op("vector", "tensor_add", ob[p][:, :], t1b[p][:, :], yf[p][:, :], inc=sA2)
            osh, osv = rOS.tick(p)
            r.op("scalar", "dma_start", out.ap()[tt * 128:(tt + 1) * 128, :], ob[p][:, :],
                 waits=[(sA2, tt + 1)], inc=osh, incv=16)
        r.op("scalar", None, waits=rOS.finals())
        with nc.Block() as blk:
            r.replay_into(blk)

    return nc


def _host_constants():
    hd = 128
    inv = 1.0 / (10000.0 ** (np.arange(0, hd, 2)[: hd // 2] / hd))
    pos = np.arange(S, dtype=np.float64)
    fr = pos[None, :] * inv[:, None]
    cos = np.cos(fr)
    sin = np.sin(fr)
    ropeKc = np.concatenate([cos, cos], axis=0).astype(np.float32)
    ropeKs = np.concatenate([-sin, sin], axis=0).astype(np.float32)
    pswap = np.zeros((128, 128), dtype=np.float32)
    for i in range(64):
        pswap[i + 64, i] = 1.0
        pswap[i, i + 64] = 1.0
    pswap = pswap.astype(ml_dtypes.bfloat16)
    onesc = np.ones((1, 128), dtype=ml_dtypes.bfloat16)
    onesk = np.ones((128, 1), dtype=ml_dtypes.bfloat16)
    return ropeKc, ropeKs, pswap, onesc, onesk


def _mask_master(width, nk, qb):
    k = np.arange(128)[:, None]
    v = np.arange(width)[None, :]
    return (k <= v + qb - (nk - 1) * 128).astype(ml_dtypes.bfloat16)


def kernel(L0_attn_x, L0_attn_Wq, L0_attn_Wk, L0_attn_Wv, L0_attn_Wo,
           L0_ffn_w_gate, L0_ffn_s_gate, L0_ffn_w_up, L0_ffn_s_up,
           L0_ffn_w_down, L0_ffn_s_down):
    x = np.ascontiguousarray(np.asarray(L0_attn_x, dtype=np.float32))
    Wq = np.ascontiguousarray(np.asarray(L0_attn_Wq, dtype=np.float32))
    Wk = np.ascontiguousarray(np.asarray(L0_attn_Wk, dtype=np.float32))
    Wv = np.ascontiguousarray(np.asarray(L0_attn_Wv, dtype=np.float32))
    Wo = np.ascontiguousarray(np.asarray(L0_attn_Wo, dtype=np.float32))
    wg = np.asarray(L0_ffn_w_gate)
    wu = np.asarray(L0_ffn_w_up)
    wd = np.asarray(L0_ffn_w_down)
    wgT = np.ascontiguousarray(wg.T)
    wuT = np.ascontiguousarray(wu.T)
    wdT = np.ascontiguousarray(wd.T)
    sg = np.ascontiguousarray(np.asarray(L0_ffn_s_gate, dtype=np.float32).reshape(NIT, 128).T)
    su = np.ascontiguousarray(np.asarray(L0_ffn_s_up, dtype=np.float32).reshape(NIT, 128).T)
    sd = np.ascontiguousarray(np.asarray(L0_ffn_s_down, dtype=np.float32).reshape(32, 128).T)

    _, _, pswap, onesc, onesk = _host_constants()

    in_maps = []
    meta = []
    for c in range(8):
        b, j = c // 4, c % 4
        qbA, qbB = j * 512, (7 - j) * 512
        x_own = np.ascontiguousarray(
            np.concatenate([x[b, qbA:qbA + 512], x[b, qbB:qbB + 512]], axis=0))
        posq = np.concatenate([np.arange(qbA, qbA + 512), np.arange(qbB, qbB + 512)])
        hd = 128
        inv = 1.0 / (10000.0 ** (np.arange(0, hd, 2)[: hd // 2] / hd))
        fr = posq[None, :].astype(np.float64) * inv[:, None]
        cosq = np.cos(fr)
        sinq = np.sin(fr)
        rqc = np.concatenate([cosq, cosq], axis=0).astype(np.float32)
        rqs = np.concatenate([-sinq, sinq], axis=0).astype(np.float32)
        mA = _mask_master(2432, 16, qbA)
        mB = _mask_master(4480, 32, qbB)
        in_maps.append(dict(
            x_own=x_own, wq=Wq, wk=Wk, wv=Wv, wo=Wo,
            wgT=wgT, wuT=wuT, wdT=wdT, sg_in=sg, su_in=su, sd_in=sd,
            ropeQc=rqc, ropeQs=rqs,
            maskA=mA, maskB=mB, pswap=pswap, onesc=onesc, onesk=onesk))
        meta.append((b, qbA, qbB))

    nc = build_nc()
    res = run_bass_kernel_spmd(nc, in_maps, core_ids=list(range(8)))

    y = np.empty((2, S, H), dtype=np.float32)
    for c in range(8):
        b, qbA, qbB = meta[c]
        oc = np.asarray(res.results[c]["out"])
        y[b, qbA:qbA + 512] = oc[0:512]
        y[b, qbB:qbB + 512] = oc[512:1024]
    return y
